# version 05 (final)
# baseline (speedup 1.0000x reference)
import numpy as np
import ml_dtypes
import concourse.bass as bass
import concourse.mybir as mybir
from concourse.bass_utils import run_bass_kernel_spmd
from contextlib import ExitStack

F32 = mybir.dt.float32
BF16 = mybir.dt.bfloat16
F16 = mybir.dt.float16
U8 = mybir.dt.uint8
AF = mybir.ActivationFunctionType
ALU = mybir.AluOpType

T = 1024
KT = 32
DIM = 4096
INTER = 8192
NG = 8
EPS = 1e-5


def mtiles():
    out = [("dt", 18432, None)]
    for g in range(NG):
        out.append(("B", 16384 + 128 * g, g))
        out.append(("C", 17408 + 128 * g, g))
        for i in range(8):
            out.append(("x", 8192 + 1024 * g + 128 * i, 16 * g + 2 * i))
    for i in range(64):
        out.append(("gate", 128 * i, i))
    return out


MTILES = mtiles()

STOP = None
SKIP_EXPERTS = False
DEBUG = False


class _StopBuild(Exception):
    pass


def _ck(B, nc, name):
    if STOP == name:
        B.flush(nc)
        raise _StopBuild


class Ch:
    def __init__(self, name, step=1):
        self.name, self.step, self.v = name, step, 0
        self.h = None

    def tick(self, n=1):
        self.v += self.step * n
        return self.v


class Bld:
    ENGS = ("g", "t", "v", "a", "s")

    def __init__(self):
        self.q = {e: [] for e in self.ENGS}

    def emit(self, eng, fn):
        self.q[eng].append(fn)

    def wait(self, eng, ch, val=None):
        v = ch.v if val is None else val
        if v > 0:
            self.emit(eng, lambda E, c=ch, vv=v: E.wait_ge(c.h, vv))

    def op(self, eng, fn, inc=None, n=1):
        if inc is None:
            self.emit(eng, fn)
            return None
        v = inc.tick(n)
        self.emit(eng, lambda E, f=fn, c=inc, k=inc.step * n: f(E).then_inc(c.h, k))
        return v

    def flush(self, nc):
        q = self.q
        self.q = {e: [] for e in self.ENGS}
        with nc.Block() as blk:
            if q["g"]:
                blk.gpsimd(lambda E, qq=q["g"]: [f(E) for f in qq])
            if q["t"]:
                blk.tensor(lambda E, qq=q["t"]: [f(E) for f in qq])
            if q["v"]:
                blk.vector(lambda E, qq=q["v"]: [f(E) for f in qq])
            if q["a"]:
                blk.scalar(lambda E, qq=q["a"]: [f(E) for f in qq])
            if q["s"]:
                blk.sync(lambda E, qq=q["s"]: [f(E) for f in qq])


def bc2(col2, width):
    return col2.unsqueeze(2).broadcast_to([128, 2, width])


def build():
    try:
        return _build()
    except _StopBuild:
        return _BUILD_NC[0]


_BUILD_NC = [None]


def _build():
    nc = bass.Bass("TRN2", target_bir_lowering=False)
    _BUILD_NC[0] = nc

    d_x0 = nc.dram_tensor("x0T", [DIM, T], F32, kind="ExternalInput").ap()
    d_xh = nc.dram_tensor("x0hT", [DIM, 4], F32, kind="ExternalInput").ap()
    d_win = nc.dram_tensor("WinT", [145, KT, 128, 128], F32, kind="ExternalInput").ap()
    d_wout = nc.dram_tensor("WoutT", [32, 64, 128, 128], F32, kind="ExternalInput").ap()
    d_wup = nc.dram_tensor("WupT", [32, 32, 128, 128], F32, kind="ExternalInput").ap()
    d_lA = nc.dram_tensor("lApad", [KT, 128, 128], F32, kind="ExternalInput").ap()
    d_lgw = nc.dram_tensor("lgw", [KT, 128, 4], F32, kind="ExternalInput").ap()
    d_lA3 = nc.dram_tensor("lA3", [KT, 128, 8], F32, kind="ExternalInput").ap()
    d_erep3 = nc.dram_tensor("erep3", [4, 8], F32, kind="ExternalInput").ap()
    d_lB = nc.dram_tensor("lBT", [4, 8, DIM], F32, kind="ExternalInput").ap()
    d_cw = nc.dram_tensor("cw4", [128, 80, 4], F32, kind="ExternalInput").ap()
    d_cb = nc.dram_tensor("cbT", [128, 80], F32, kind="ExternalInput").ap()
    d_dtb = nc.dram_tensor("dtbias", [128, 1], F32, kind="ExternalInput").ap()
    d_alog = nc.dram_tensor("alog", [128, 1], F32, kind="ExternalInput").ap()
    d_dvec = nc.dram_tensor("dvec", [1, 128], F32, kind="ExternalInput").ap()
    d_nw = nc.dram_tensor("nwT", [128, 64], F32, kind="ExternalInput").ap()
    d_eb = nc.dram_tensor("ebias", [4, 1], F32, kind="ExternalInput").ap()
    d_flag = nc.dram_tensor("flag", [128, 1], F32, kind="ExternalInput").ap()
    d_triU = nc.dram_tensor("triU", [128, 128], F32, kind="ExternalInput").ap()
    d_idf = nc.dram_tensor("identf", [128, 128], F32, kind="ExternalInput").ap()
    d_caus = nc.dram_tensor("causal", [128, 128], U8, kind="ExternalInput").ap()
    d_ones = nc.dram_tensor("ones128", [128, 128], F32, kind="ExternalInput").ap()
    d_o1r = nc.dram_tensor("ones1r", [1, 128], F32, kind="ExternalInput").ap()
    d_erep = nc.dram_tensor("erep2", [4, 128], F32, kind="ExternalInput").ap()

    d_out = nc.dram_tensor("o", [KT, 128, T], F32, kind="ExternalOutput").ap()

    d_gate = nc.dram_tensor("gate_d", [64, 128, T], F16).ap()
    d_yp = nc.dram_tensor("ypart_d", [64, 128, 8, 128], F16).ap()
    d_yf = nc.dram_tensor("yfin_d", [64, 8, 128, 128], F16).ap()
    d_mo = nc.dram_tensor("mout_d", [KT, 128, T], F32).ap()
    d_x1 = nc.dram_tensor("x1_d", [KT, 128, T], F32).ap()
    d_ff = nc.dram_tensor("ffn_d", [KT, 128, T], F32).ap()
    d_cci = nc.dram_tensor("cc_in", [128, INTER], F32).ap()
    d_cco = nc.dram_tensor("cc_out", [256, INTER], F32).ap()

    pe, dv, ac = Ch("pe"), Ch("dv"), Ch("ac")
    wd, ld, st = Ch("wd", 16), Ch("ld", 16), Ch("st", 16)
    tr, cc = Ch("tr", 16), Ch("cc", 1)
    ALLCH = (pe, dv, ac, wd, ld, st, tr, cc)
    B = Bld()

    def LNfin(psums, ac_ch=ac):
        (ps0, ps1, pq0, pq1) = psums
        for h0, psx, pqx in ((0, ps0, pq0), (512, ps1, pq1)):
            B.op("v", lambda E, h0=h0, p=psx: E.tensor_scalar(
                mu_bc[:, h0:h0 + 512], p[:, :], 1.0 / DIM, None, ALU.mult))
            B.op("v", lambda E, h0=h0, p=pqx: E.tensor_scalar(
                var_t[:, h0:h0 + 512], p[:, :], 1.0 / DIM, None, ALU.mult))
            B.op("v", lambda E, h0=h0: E.tensor_tensor(
                lnv_t[:, h0:h0 + 512], mu_bc[:, h0:h0 + 512], mu_bc[:, h0:h0 + 512], ALU.mult))
            B.op("v", lambda E, h0=h0: E.tensor_tensor(
                var_t[:, h0:h0 + 512], var_t[:, h0:h0 + 512], lnv_t[:, h0:h0 + 512],
                ALU.subtract), inc=dv)
        B.wait("a", dv)
        B.op("a", lambda E: E.activation(lnv_t[:, :], var_t[:, :], AF.Ln, bias=epscol[:, :]))
        B.op("a", lambda E: E.activation(r_bc[:, :], lnv_t[:, :], AF.Exp, scale=-0.5), ac)
        return ac.v

    with ExitStack() as GS:
        eg = GS.enter_context
        for chn in ALLCH:
            chn.h = eg(nc.semaphore(chn.name))

        triU = eg(nc.sbuf_tensor("triU_s", [128, 128], F32))
        idf = eg(nc.sbuf_tensor("idf_s", [128, 128], F32))
        caus = eg(nc.sbuf_tensor("caus_s", [128, 128], U8))
        ones = eg(nc.sbuf_tensor("ones_s", [128, 128], BF16))
        o1r = eg(nc.sbuf_tensor("o1r_s", [1, 128], F32))
        zerot = eg(nc.sbuf_tensor("zerot_s", [128, 128], BF16))
        nwc = eg(nc.sbuf_tensor("nwc_s", [128, 64], F32))
        dbias = eg(nc.sbuf_tensor("dbias_s", [128, 1], F32))
        alogc = eg(nc.sbuf_tensor("alog_s", [128, 1], F32))
        expA = eg(nc.sbuf_tensor("expA_s", [128, 1], F32))
        flagc = eg(nc.sbuf_tensor("flag_s", [128, 1], F32))
        dbc = eg(nc.sbuf_tensor("dbc_s", [128, 128], F32))
        drow = eg(nc.sbuf_tensor("drow_s", [1, 128], F32))
        ebc = eg(nc.sbuf_tensor("ebc_s", [4, 1], F32))
        erep = eg(nc.sbuf_tensor("erep_s", [4, 128], BF16))
        erep3 = eg(nc.sbuf_tensor("erep3_s", [4, 8], BF16))
        cw4 = eg(nc.sbuf_tensor("cw4_s", [128, 320], F32))
        cbT = eg(nc.sbuf_tensor("cbT_s", [128, 80], F32))
        mu_bc = eg(nc.sbuf_tensor("mu_bc", [128, T], F32))
        r_bc = eg(nc.sbuf_tensor("r_bc", [128, T], F32))
        var_t = eg(nc.sbuf_tensor("var_t", [128, T], F32))
        lnv_t = eg(nc.sbuf_tensor("lnv_t", [128, T], F32))
        ssq_tm = eg(nc.sbuf_tensor("ssq_tm", [128, 8], F32))
        epscol = eg(nc.sbuf_tensor("epscol", [128, 1], F32))
        onecol = eg(nc.sbuf_tensor("onecol", [128, 1], F32))

        def vsp():
            B.op("v", lambda E: E.tensor_copy(zerot[:, 0:64], zerot[:, 64:128]))

        for dst, src in ((triU, d_triU), (idf, d_idf), (caus, d_caus),
                         (ones, d_ones), (o1r, d_o1r), (nwc, d_nw),
                         (dbias, d_dtb), (alogc, d_alog), (flagc, d_flag),
                         (drow, d_dvec), (ebc, d_eb), (erep, d_erep),
                         (erep3, d_erep3),
                         (cw4, d_cw.rearrange("p a b -> p (a b)")), (cbT, d_cb)):
            B.op("g", lambda E, ds=dst, sr=src: E.dma_start(out=ds[:, :], in_=sr), ld)
        B.op("v", lambda E: E.memset(zerot[:, :], 0.0))
        B.op("v", lambda E: E.memset(epscol[:, :], EPS))
        B.op("v", lambda E: E.memset(onecol[:, :], 1.0))
        B.op("v", lambda E: E.memset(ssq_tm[:, :], 0.0), dv)
        B.wait("a", ld)
        B.op("a", lambda E: E.activation(expA[:, :], alogc[:, :], AF.Exp), ac)
        consts_ac = ac.v

        MS = ExitStack()
        em = MS.enter_context
        xln = em(nc.sbuf_tensor("xln", [128, KT * T], BF16))
        xhln = em(nc.sbuf_tensor("xhln", [128, KT * 4], BF16))
        cfm = em(nc.sbuf_tensor("cfm", [128, NG * T], BF16))
        s_all = em(nc.sbuf_tensor("s_all", [128, INTER], F32))
        a_tm = em(nc.sbuf_tensor("a_tm", [128, T], F32))
        ncum_tm = em(nc.sbuf_tensor("ncum_tm", [128, T], F32))
        dtb_tm = em(nc.sbuf_tensor("dtb_tm", [128, T], F32))
        w_all = em(nc.sbuf_tensor("w_all", [128, T], BF16))
        exl = em(nc.sbuf_tensor("exl", [128, T], F32))
        dec = em(nc.sbuf_tensor("dec", [128, T], F32))
        nct = em(nc.sbuf_tensor("nct", [128, 8], F32))
        muh = em(nc.sbuf_tensor("muh", [128, 4], F32))
        rh = em(nc.sbuf_tensor("rh", [128, 4], F32))
        varh = em(nc.sbuf_tensor("varh", [128, 4], F32))
        pxn = em(nc.sbuf_tensor("pxn", [128, 8], F32))

        with ExitStack() as PS:
            ep = PS.enter_context
            sq2 = [ep(nc.sbuf_tensor(f"sq2{i}", [128, T], BF16)) for i in range(2)]
            hfold = ep(nc.sbuf_tensor("hfold", [128, 8], F32))
            lntmp = ep(nc.sbuf_tensor("lntmp", [128, T], F32))
            tmph = ep(nc.sbuf_tensor("tmph", [128, 4], F32))
            p_s0 = ep(nc.psum_tensor("p_s0", [128, 512], F32))
            p_s1 = ep(nc.psum_tensor("p_s1", [128, 512], F32))
            p_q0 = ep(nc.psum_tensor("p_q0", [128, 512], F32))
            p_q1 = ep(nc.psum_tensor("p_q1", [128, 512], F32))
            p_hs = ep(nc.psum_tensor("p_hs", [128, 512], F32))

            B.op("g", lambda E: E.dma_start(
                out=xln[:, :].rearrange("p (k t) -> p k t", k=KT),
                in_=d_x0.rearrange("(k p) t -> k p t", p=128).transpose([1, 0, 2])), ld)
            B.op("g", lambda E: E.dma_start(
                out=xhln[:, :].rearrange("p (k t) -> p k t", k=KT),
                in_=d_xh.rearrange("(k p) t -> k p t", p=128).transpose([1, 0, 2])), ld)
            x0_ld = ld.v
            B.wait("a", ld, x0_ld)
            B.wait("v", ld, x0_ld)
            onesf = lntmp[:, 256:384]
            B.op("v", lambda E: E.memset(onesf, 1.0))
            sq128 = lntmp[:, 0:128]
            hf64 = lntmp[:, 128:192]
            hf32 = lntmp[:, 192:224]
            hf16 = lntmp[:, 224:240]
            hf8 = lntmp[:, 240:248]
            B.op("v", lambda E: E.tensor_tensor(hf64, xhln[:, 0:64], xhln[:, 64:128], ALU.add))
            vsp()
            B.op("v", lambda E: E.tensor_tensor(hf32, hf64[:, 0:32], hf64[:, 32:64], ALU.add))
            vsp()
            B.op("v", lambda E: E.tensor_tensor(hf16, hf32[:, 0:16], hf32[:, 16:32], ALU.add))
            vsp()
            B.op("v", lambda E: E.tensor_tensor(hf8, hf16[:, 0:8], hf16[:, 8:16], ALU.add))
            vsp()
            B.op("v", lambda E: E.tensor_tensor(hfold[:, 0:4], hf8[:, 0:4], hf8[:, 4:8], ALU.add))
            B.op("v", lambda E: E.tensor_tensor(sq128[:, :], xhln[:, :], xhln[:, :], ALU.mult))
            vsp()
            B.op("v", lambda E: E.tensor_tensor(hf64, sq128[:, 0:64], sq128[:, 64:128], ALU.add))
            vsp()
            B.op("v", lambda E: E.tensor_tensor(hf32, hf64[:, 0:32], hf64[:, 32:64], ALU.add))
            vsp()
            B.op("v", lambda E: E.tensor_tensor(hf16, hf32[:, 0:16], hf32[:, 16:32], ALU.add))
            vsp()
            B.op("v", lambda E: E.tensor_tensor(hf8, hf16[:, 0:8], hf16[:, 8:16], ALU.add))
            vsp()
            B.op("v", lambda E: E.tensor_tensor(hfold[:, 4:8], hf8[:, 0:4], hf8[:, 4:8], ALU.add), inc=dv)
            hfold_dv = dv.v
            B.wait("t", ld, x0_ld)
            for k in range(KT):
                s0, s1 = (k == 0), (k == KT - 1)
                B.op("t", lambda E, k=k, s0=s0, s1=s1: E.matmul(
                    p_s0[:, :], ones[:, :], xln[:, k * T:k * T + 512], start=s0, stop=s1))
            for k in range(KT):
                s0, s1 = (k == 0), (k == KT - 1)
                B.op("t", lambda E, k=k, s0=s0, s1=s1: E.matmul(
                    p_s1[:, :], ones[:, :], xln[:, k * T + 512:(k + 1) * T], start=s0, stop=s1))
            B.wait("t", dv, hfold_dv)
            B.op("t", lambda E: E.matmul(
                p_hs[:, 0:8], onesf, hfold[:, 0:8], start=True, stop=True))
            sq_pe = [0, 0]
            for k in range(KT):
                par = k % 2
                s0, s1 = (k == 0), (k == KT - 1)
                B.wait("a", pe, sq_pe[par])
                B.op("a", lambda E, k=k, par=par: E.activation(
                    sq2[par][:, :], xln[:, k * T:(k + 1) * T], AF.Square), ac)
                B.wait("t", ac)
                B.op("t", lambda E, par=par, s0=s0, s1=s1: E.matmul(
                    p_q0[:, :], ones[:, :], sq2[par][:, 0:512], start=s0, stop=s1))
                B.op("t", lambda E, par=par, s0=s0, s1=s1: E.matmul(
                    p_q1[:, :], ones[:, :], sq2[par][:, 512:T], start=s0, stop=s1), inc=pe)
                sq_pe[par] = pe.v
            stats_pe = pe.v

            B.wait("v", pe, stats_pe)
            ln1_ac = LNfin((p_s0, p_s1, p_q0, p_q1))
            inv = 1.0 / DIM
            B.op("v", lambda E: E.tensor_scalar(muh[:, :], p_hs[:, 0:4], inv, None, ALU.mult))
            B.op("v", lambda E: E.tensor_scalar(varh[:, :], p_hs[:, 4:8], inv, None, ALU.mult))
            vsp()
            B.op("v", lambda E: E.tensor_tensor(tmph[:, :], muh[:, :], muh[:, :], ALU.mult))
            vsp()
            B.op("v", lambda E: E.tensor_tensor(varh[:, :], varh[:, :], tmph[:, :], ALU.subtract), dv)
            B.wait("a", dv)
            B.op("a", lambda E: E.activation(tmph[:, :], varh[:, :], AF.Ln, bias=epscol[:, :]))
            B.op("a", lambda E: E.activation(rh[:, :], tmph[:, :], AF.Exp, scale=-0.5), ac)
            rh_ac = ac.v

            B.wait("v", ac, max(ln1_ac, rh_ac))
            if DEBUG:
                d_he = nc.dram_tensor("dbg_E", [128, 12], F32, kind="ExternalOutput").ap()
                d_hf = nc.dram_tensor("dbg_hf", [128, 8], F32, kind="ExternalOutput").ap()
                d_of = nc.dram_tensor("dbg_of", [128, 128], F32, kind="ExternalOutput").ap()
                B.wait("g", ac, max(ln1_ac, rh_ac))
                B.op("g", lambda E: E.dma_start(out=d_he[:, 0:4], in_=muh[:, :]), st)
                B.op("g", lambda E: E.dma_start(out=d_he[:, 4:8], in_=varh[:, :]), st)
                B.op("g", lambda E: E.dma_start(out=d_he[:, 8:12], in_=rh[:, :]), st)
                B.op("g", lambda E: E.dma_start(out=d_hf, in_=hfold[:, :]), st)
                B.op("g", lambda E: E.dma_start(out=d_of, in_=lntmp[:, 256:384]), st)
            for k in range(KT):
                B.op("v", lambda E, k=k: E.tensor_tensor(
                    lntmp[:, :], xln[:, k * T:(k + 1) * T], mu_bc[:, :], ALU.subtract))
                B.op("v", lambda E, k=k: E.tensor_tensor(
                    xln[:, k * T:(k + 1) * T], lntmp[:, :], r_bc[:, :], ALU.mult))
            B.op("v", lambda E: E.tensor_tensor(
                lntmp[:, 0:128].rearrange("p (k t) -> p k t", k=KT),
                xhln[:, :].rearrange("p (k t) -> p k t", k=KT),
                muh[:, :].unsqueeze(1).broadcast_to([128, KT, 4]), ALU.subtract))
            B.op("v", lambda E: E.tensor_tensor(
                xhln[:, :].rearrange("p (k t) -> p k t", k=KT),
                lntmp[:, 0:128].rearrange("p (k t) -> p k t", k=KT),
                rh[:, :].unsqueeze(1).broadcast_to([128, KT, 4]), ALU.mult))
            B.op("v", lambda E: E.memset(s_all[:, :], 0.0), dv)
            xln_dv = dv.v
            if DEBUG:
                d_hr = nc.dram_tensor("dbg_xhraw", [128, KT * 4], BF16, kind="ExternalOutput").ap()
                B.wait("g", dv, xln_dv)
                B.op("g", lambda E: E.dma_start(out=d_hr, in_=xhln[:, :]), st)
            B.flush(nc)
            _ck(B, nc, "p12")

        with ExitStack() as PS:
            ep = PS.enter_context
            wsl = [ep(nc.sbuf_tensor(f"wsl{i}", [128, KT * 128], BF16)) for i in range(2)]
            cv_ext = ep(nc.sbuf_tensor("cv_ext", [128, 1028], F32))
            cv_acc = ep(nc.sbuf_tensor("cv_acc", [128, T], F32))
            scr1 = ep(nc.sbuf_tensor("scr1", [128, 128], F32))
            scr2 = ep(nc.sbuf_tensor("scr2", [128, 128], F32))
            eccol = ep(nc.sbuf_tensor("eccol", [128, 1], F32))
            DS = ExitStack()
            dtraw = DS.enter_context(nc.sbuf_tensor("dtraw", [128, T], F32))
            dtbfm = DS.enter_context(nc.sbuf_tensor("dtbfm", [128, T], F32))
            nafm = DS.enter_context(nc.sbuf_tensor("nafm", [128, T], F32))
            pA = ep(nc.psum_tensor("pA", [128, 512], F32))
            pB = ep(nc.psum_tensor("pB", [128, 512], F32))
            pH = ep(nc.psum_tensor("pH", [128, 512], F32))
            pT1 = ep(nc.psum_tensor("pT1", [128, 512], F32))
            pY = [ep(nc.psum_tensor(f"pY{i}", [128, 512], F32)) for i in range(2)]
            pG = ep(nc.psum_tensor("pG", [128, 512], F32))
            pX = ep(nc.psum_tensor("pX", [128, 512], F32))

            slab_free = [0, 0]
            slab_ld = [0, 0]

            def load_slab(mi):
                if mi >= 145:
                    return
                buf = mi % 2
                B.wait("g", pe, slab_free[buf])
                B.op("g", lambda E, mi=mi, buf=buf: E.dma_start(
                    out=wsl[buf][:, :].rearrange("p (k m) -> p k m", k=KT),
                    in_=d_win[mi].transpose([1, 0, 2])), wd)
                slab_ld[buf] = wd.v

            pAB_free = [dv, 0]

            def proj_matmuls(mi, halo):
                buf = mi % 2
                B.wait("t", wd, slab_ld[buf])
                B.wait("t", pAB_free[0], pAB_free[1])
                for k in range(KT):
                    s0, s1 = (k == 0), (k == KT - 1)
                    last = s1 and not halo
                    B.op("t", lambda E, k=k, buf=buf, s0=s0, s1=s1: E.matmul(
                        pA[:, :], wsl[buf][:, k * 128:(k + 1) * 128],
                        xln[:, k * T:k * T + 512], start=s0, stop=s1))
                    B.op("t", lambda E, k=k, buf=buf, s0=s0, s1=s1: E.matmul(
                        pB[:, :], wsl[buf][:, k * 128:(k + 1) * 128],
                        xln[:, k * T + 512:(k + 1) * T], start=s0, stop=s1),
                        inc=(pe if last else None))
                    if halo:
                        B.op("t", lambda E, k=k, buf=buf, s0=s0, s1=s1: E.matmul(
                            pH[:, 0:4], wsl[buf][:, k * 128:(k + 1) * 128],
                            xhln[:, 4 * k:4 * k + 4], start=s0, stop=s1),
                            inc=(pe if s1 else None))
                slab_free[buf] = pe.v
                return pe.v

            def evac_and_conv(j80):
                B.op("v", lambda E: E.tensor_copy(cv_ext[:, 4:516], pA[:, :]))
                B.op("v", lambda E: E.tensor_copy(cv_ext[:, 516:1028], pB[:, :]))
                B.op("v", lambda E: E.tensor_copy(cv_ext[:, 0:4], pH[:, 0:4]), inc=dv)
                pAB_free[1] = dv.v
                B.wait("v", ac)
                B.op("v", lambda E, j=j80: E.tensor_scalar(
                    cv_acc[:, :], cv_ext[:, 1:1025], cw4[:, 4 * j:4 * j + 1], None, ALU.mult))
                for kk in range(1, 4):
                    B.op("v", lambda E, j=j80, kk=kk: E.scalar_tensor_tensor(
                        cv_acc[:, :], cv_ext[:, 1 + kk:1025 + kk],
                        cw4[:, 4 * j + kk:4 * j + kk + 1], cv_acc[:, :],
                        op0=ALU.mult, op1=ALU.add), inc=dv)
                return dv.v

            def silu_to(ap_fn, j80):
                B.wait("a", dv)
                B.op("a", lambda E, j=j80, f=ap_fn: E.activation(
                    f(), cv_acc[:, :], AF.Silu, bias=cbT[:, j:j + 1]), ac)
                return ac.v

            load_slab(0)
            load_slab(1)
            B.wait("t", dv, xln_dv)

            proj_matmuls(0, halo=False)
            load_slab(2)
            B.wait("v", pe)
            B.op("v", lambda E: E.tensor_copy(dtraw[:, 0:512], pA[:, :]))
            B.op("v", lambda E: E.tensor_copy(dtraw[:, 512:T], pB[:, :]), inc=dv)
            pAB_free[1] = dv.v
            B.op("v", lambda E: E.tensor_scalar(
                dtraw[:, :], dtraw[:, :], dbias[:, :], None, ALU.add), dv)
            B.wait("a", dv)
            B.op("a", lambda E: E.activation(cv_acc[:, :], dtraw[:, :], AF.Abs))
            B.op("a", lambda E: E.activation(cv_acc[:, :], cv_acc[:, :], AF.Exp, scale=-1.0))
            B.op("a", lambda E: E.activation(cv_acc[:, :], cv_acc[:, :], AF.Ln, bias=onecol[:, :]))
            B.op("a", lambda E: E.activation(dtbfm[:, :], dtraw[:, :], AF.Relu), ac)
            B.wait("v", ac)
            B.op("v", lambda E: E.tensor_tensor(dtbfm[:, :], dtbfm[:, :], cv_acc[:, :], ALU.add))
            B.wait("v", ac, consts_ac)
            B.op("v", lambda E: E.tensor_scalar(
                nafm[:, :], dtbfm[:, :], expA[:, :], None, ALU.mult), dv)

            na_dv = dv.v
            B.wait("a", dv, na_dv)
            for c in range(8):
                B.op("a", lambda E, c=c: E.activation(
                    cv_acc[:, 0:128], nafm[:, c * 128:(c + 1) * 128], AF.Copy,
                    accum_out=nct[:, c:c + 1]), ac)
            nct_ac = ac.v
            for c in range(8):
                cl = c * 128
                B.op("v", lambda E, cl=cl: E.tensor_tensor_scan(
                    scr1[:, :], nafm[:, cl:cl + 128], nafm[:, cl:cl + 128],
                    0.0, op0=ALU.add, op1=ALU.bypass), inc=dv)
                B.wait("t", dv)
                B.op("t", lambda E: E.transpose(pX[:, 0:128], scr1[:, :], idf[:, :]), pe)
                B.wait("v", pe)
                B.op("v", lambda E, cl=cl: E.tensor_copy(ncum_tm[:, cl:cl + 128], pX[:, 0:128]), dv)
                B.wait("t", dv)
                B.op("t", lambda E, cl=cl: E.transpose(
                    pX[:, 128:256], dtbfm[:, cl:cl + 128], idf[:, :]), pe)
                B.wait("v", pe)
                B.op("v", lambda E, cl=cl: E.tensor_copy(dtb_tm[:, cl:cl + 128], pX[:, 128:256]), dv)
                B.wait("t", dv)
                B.op("t", lambda E, cl=cl: E.transpose(
                    pX[:, 256:384], nafm[:, cl:cl + 128], idf[:, :]), pe)
                B.wait("v", pe)
                B.op("v", lambda E, cl=cl: E.tensor_scalar(
                    a_tm[:, cl:cl + 128], pX[:, 256:384], -1.0, None, ALU.mult), dv)
                atm_dv = dv.v
                B.wait("t", dv, atm_dv)
                B.wait("t", ac, nct_ac)
                B.op("t", lambda E, c=c: E.matmul(
                    pG[:, 0:128], nct[:, c:c + 1].broadcast_to([128, 128]), idf[:, :],
                    start=True, stop=True), pe)
                B.wait("v", pe)
                B.op("v", lambda E, cl=cl: E.tensor_tensor(
                    scr2[:, :], ncum_tm[:, cl:cl + 128], pG[:, 0:128], ALU.subtract), dv)
                B.wait("a", dv)
                B.op("a", lambda E, cl=cl: E.activation(w_all[:, cl:cl + 128], scr2[:, :], AF.Exp))
                B.op("a", lambda E, cl=cl: E.activation(
                    exl[:, cl:cl + 128], ncum_tm[:, cl:cl + 128], AF.Exp, scale=-1.0))
                B.op("a", lambda E, c=c: E.activation(
                    eccol[:, :], nct[:, c:c + 1], AF.Exp, scale=-1.0), ac)
                B.wait("t", ac)
                B.op("t", lambda E: E.matmul(
                    pG[:, 128:256], eccol[:, :].broadcast_to([128, 128]), idf[:, :],
                    start=True, stop=True), pe)
                B.wait("v", pe)
                B.op("v", lambda E, cl=cl: E.tensor_copy(dec[:, cl:cl + 128], pG[:, 128:256]), dv)
            B.wait("t", dv)
            B.op("t", lambda E: E.matmul(pG[:, 256:384], o1r[:, :], drow[:, :],
                                         start=True, stop=True), pe)
            B.wait("v", pe)
            B.op("v", lambda E: E.tensor_copy(dbc[:, :], pG[:, 256:384]), dv)
            prep_dv = dv.v
            prep_ac = ac.v

            DS.close()
            bfm = ep(nc.sbuf_tensor("bfm", [128, T], BF16))
            btm = ep(nc.sbuf_tensor("btm", [128, T], BF16))
            gt_sb = ep(nc.sbuf_tensor("gt_sb", [128, T], BF16))
            xcbf = ep(nc.sbuf_tensor("xcbf", [128, T], BF16))
            xtm = [ep(nc.sbuf_tensor(f"xtm{i}", [128, T], BF16)) for i in range(2)]
            xdt = [ep(nc.sbuf_tensor(f"xdt{i}", [128, T], BF16)) for i in range(2)]
            bd2 = [ep(nc.sbuf_tensor(f"bd2{i}", [128, 256], BF16)) for i in range(2)]
            ysb = [ep(nc.sbuf_tensor(f"ysb{i}", [128, T], F32)) for i in range(2)]
            gsb = [ysb[i][:, :].bitcast(F16) for i in range(2)]
            sbf = [ep(nc.sbuf_tensor(f"sbf{i}", [128, 64], BF16)) for i in range(2)]
            exh = [ep(nc.sbuf_tensor(f"exh{i}", [128, 128], BF16)) for i in range(2)]
            tgb0 = ep(nc.sbuf_tensor("tgb0", [128, 128], BF16))
            tgb = [tgb0, tgb0]
            mtb = [ep(nc.sbuf_tensor(f"mtb{i}", [128, 128], BF16)) for i in range(2)]
            t2b = scr1
            tyb = scr2
            _ck(B, nc, "dt")
            y_free = {0: 0, 1: 0}
            xtm_free = [0, 0]
            ysb_store = [0, 0]
            prev_mt = [0]
            pending = None
            xidx = 0
            hbc_j = 0

            def emit_ssm(par, h0, g, trv):
                nonlocal y_free
                B.wait("v", ac, prep_ac)
                B.wait("v", st, ysb_store[par])
                for c in range(8):
                    cl = c * 128
                    B.wait("v", tr, trv[c])
                    B.op("v", lambda E, par=par, cl=cl, h0=h0: E.tensor_tensor(
                        xdt[par][:, cl:cl + 128].rearrange("p (h q) -> p h q", h=2),
                        xtm[par][:, cl:cl + 128].rearrange("p (h q) -> p h q", h=2),
                        bc2(dtb_tm[:, cl + h0:cl + h0 + 2], 64), ALU.mult))
                    B.op("v", lambda E, par=par, cl=cl, h0=h0: E.tensor_tensor(
                        bd2[par][:, :].rearrange("p (h q) -> p h q", h=2),
                        btm[:, cl:cl + 128].unsqueeze(1).broadcast_to([128, 2, 128]),
                        bc2(w_all[:, cl + h0:cl + h0 + 2], 128), ALU.mult), inc=dv)
                    if c > 0:
                        B.op("v", lambda E, h0=h0: E.tensor_copy(
                            sbf[0][:, :], s_all[:, 64 * h0:64 * h0 + 64]))
                        B.op("v", lambda E, h0=h0: E.tensor_copy(
                            sbf[1][:, :], s_all[:, 64 * h0 + 64:64 * h0 + 128]), inc=dv)
                    in_dv = dv.v
                    B.wait("t", dv, in_dv)
                    B.wait("t", ac)
                    B.op("t", lambda E, cl=cl, h0=h0: E.matmul(
                        pT1[:, 0:128],
                        a_tm[:, cl + h0:cl + h0 + 1].broadcast_to([128, 128]),
                        triU[:, :], start=True, stop=True))
                    B.op("t", lambda E, cl=cl, h0=h0: E.matmul(
                        pT1[:, 128:256],
                        a_tm[:, cl + h0 + 1:cl + h0 + 2].broadcast_to([128, 128]),
                        triU[:, :], start=True, stop=True), inc=pe)
                    t1_pe = pe.v
                    B.wait("a", pe, t1_pe)
                    B.wait("a", dv, prev_mt[0])
                    B.op("a", lambda E, cl=cl, h0=h0: E.activation(
                        exh[0][:, :], pT1[:, 0:128], AF.Exp,
                        bias=ncum_tm[:, cl + h0:cl + h0 + 1]))
                    B.op("a", lambda E, cl=cl, h0=h0: E.activation(
                        exh[1][:, :], pT1[:, 128:256], AF.Exp,
                        bias=ncum_tm[:, cl + h0 + 1:cl + h0 + 2]), inc=ac)
                    ex_ac = ac.v
                    B.wait("v", ac, ex_ac)
                    for hh in range(2):
                        B.op("v", lambda E, hh=hh, cl=cl: E.tensor_tensor(
                            tgb[hh][:, :], gt_sb[:, cl:cl + 128], exh[hh][:, :], ALU.mult))
                        B.op("v", lambda E, hh=hh: E.select(
                            mtb[hh][:, :], caus[:, :], tgb[hh][:, :], zerot[:, :]), inc=dv)
                    mt_dv = dv.v
                    prev_mt[0] = mt_dv
                    bank = pY[c % 2]
                    B.wait("t", dv, max(mt_dv, y_free[c % 2]))
                    if c > 0:
                        B.op("t", lambda E, bank=bank, g=g, cl=cl: E.matmul(
                            bank[:, 0:64], cfm[:, g * T + cl:g * T + cl + 128],
                            sbf[0][:, :], start=True, stop=True))
                        B.op("t", lambda E, bank=bank, g=g, cl=cl: E.matmul(
                            bank[:, 64:128], cfm[:, g * T + cl:g * T + cl + 128],
                            sbf[1][:, :], start=True, stop=True))
                    B.op("t", lambda E, bank=bank, par=par, cl=cl: E.matmul(
                        bank[:, 128:192], mtb[0][:, :], xdt[par][:, cl:cl + 64],
                        start=True, stop=True))
                    B.op("t", lambda E, bank=bank, par=par, cl=cl: E.matmul(
                        bank[:, 192:256], mtb[1][:, :], xdt[par][:, cl + 64:cl + 128],
                        start=True, stop=True))
                    B.op("t", lambda E, bank=bank, par=par, cl=cl: E.matmul(
                        bank[:, 256:320], bd2[par][:, 0:128], xdt[par][:, cl:cl + 64],
                        start=True, stop=True))
                    B.op("t", lambda E, bank=bank, par=par, cl=cl: E.matmul(
                        bank[:, 320:384], bd2[par][:, 128:256], xdt[par][:, cl + 64:cl + 128],
                        start=True, stop=True), inc=pe)
                    y_pe = pe.v
                    B.wait("v", pe, y_pe)
                    for hh in range(2):
                        q = 64 * hh
                        if c > 0:
                            B.op("v", lambda E, bank=bank, q=q, cl=cl, h0=h0, hh=hh:
                                 E.tensor_scalar(
                                     t2b[:, q:q + 64], bank[:, q:q + 64],
                                     exl[:, cl + h0 + hh:cl + h0 + hh + 1], None, ALU.mult))
                            B.op("v", lambda E, bank=bank, q=q, par=par, cl=cl, h0=h0, hh=hh:
                                 E.scalar_tensor_tensor(
                                     tyb[:, q:q + 64], xtm[par][:, cl + q:cl + q + 64],
                                     dbc[:, h0 + hh:h0 + hh + 1], bank[:, 128 + q:192 + q],
                                     op0=ALU.mult, op1=ALU.add))
                            B.op("v", lambda E, par=par, q=q, cl=cl: E.tensor_tensor(
                                ysb[par][:, cl + q:cl + q + 64], tyb[:, q:q + 64],
                                t2b[:, q:q + 64], ALU.add))
                        else:
                            B.op("v", lambda E, bank=bank, q=q, par=par, cl=cl, h0=h0, hh=hh:
                                 E.scalar_tensor_tensor(
                                     ysb[par][:, cl + q:cl + q + 64],
                                     xtm[par][:, cl + q:cl + q + 64],
                                     dbc[:, h0 + hh:h0 + hh + 1], bank[:, 128 + q:192 + q],
                                     op0=ALU.mult, op1=ALU.add))
                        B.op("v", lambda E, bank=bank, q=q, cl=cl, h0=h0, hh=hh:
                             E.scalar_tensor_tensor(
                                 s_all[:, 64 * h0 + q:64 * h0 + q + 64],
                                 s_all[:, 64 * h0 + q:64 * h0 + q + 64],
                                 dec[:, cl + h0 + hh:cl + h0 + hh + 1],
                                 bank[:, 256 + q:320 + q], op0=ALU.mult, op1=ALU.add),
                             inc=dv)
                    y_free[c % 2] = dv.v
                tail = dv.v
                xtm_free[par] = tail
                return tail

            for g in range(NG):
                mi_b = 1 + g * 10
                proj_matmuls(mi_b, halo=True)
                load_slab(mi_b + 2)
                B.wait("v", pe)
                if pending is not None:
                    ptail = emit_ssm(*pending)
                    B.wait("g", dv, ptail)
                    B.op("g", lambda E, J=xidx - 1, par=pending[0]: E.dma_start(
                        out=d_yp[J],
                        in_=ysb[par][:, :].rearrange("l (c p) -> l c p", c=8)), st)
                    ysb_store[pending[0]] = st.v
                    pending = None
                evac_and_conv(hbc_j)
                silu_to(lambda: bfm[:, :], hbc_j)
                bfm_ac = ac.v
                hbc_j += 1
                proj_matmuls(mi_b + 1, halo=True)
                load_slab(mi_b + 3)
                B.wait("v", pe)
                evac_and_conv(hbc_j)
                silu_to(lambda g=g: cfm[:, g * T:(g + 1) * T], hbc_j)
                cfm_ac = ac.v
                hbc_j += 1
                B.wait("s", ac, bfm_ac)
                B.wait("s", dv)
                for c in range(8):
                    B.op("s", lambda E, c=c: E.dma_start(
                        out=btm[:, c * 128:(c + 1) * 128],
                        in_=bfm[:, c * 128:(c + 1) * 128], transpose=True), tr)
                btm_tr = tr.v
                B.wait("t", ac, cfm_ac)
                B.wait("t", dv)
                for c in range(8):
                    B.op("t", lambda E, c=c, g=g: E.matmul(
                        pG[:, 0:128], bfm[:, c * 128:(c + 1) * 128],
                        cfm[:, g * T + c * 128:g * T + (c + 1) * 128],
                        start=True, stop=True), pe)
                    B.wait("v", pe)
                    B.op("v", lambda E, c=c: E.tensor_copy(
                        gt_sb[:, c * 128:(c + 1) * 128], pG[:, 0:128]), dv)
                    if c < 7:
                        B.wait("t", dv)

                for i in range(8):
                    mi = mi_b + 2 + i
                    par = xidx % 2
                    h0 = 16 * g + 2 * i
                    proj_matmuls(mi, halo=True)
                    load_slab(mi + 2)
                    B.wait("v", pe)
                    if pending is not None:
                        ptail = emit_ssm(*pending)
                        B.wait("g", dv, ptail)
                        B.op("g", lambda E, J=xidx - 1, par=pending[0]: E.dma_start(
                            out=d_yp[J],
                            in_=ysb[par][:, :].rearrange("l (c p) -> l c p", c=8)), st)
                        ysb_store[pending[0]] = st.v
                        pending = None
                    evac_and_conv(hbc_j)
                    silu_to(lambda: xcbf[:, :], hbc_j)
                    silu_ac = ac.v
                    hbc_j += 1
                    B.wait("s", ac, silu_ac)
                    B.wait("s", dv, xtm_free[par])
                    trv = []
                    for c in range(8):
                        B.op("s", lambda E, c=c, par=par: E.dma_start(
                            out=xtm[par][:, c * 128:(c + 1) * 128],
                            in_=xcbf[:, c * 128:(c + 1) * 128], transpose=True), tr)
                        trv.append(tr.v)
                    pending = (par, h0, g, trv)
                    xidx += 1

            ptail = emit_ssm(*pending)
            B.wait("g", dv, ptail)
            B.op("g", lambda E, J=xidx - 1, par=pending[0]: E.dma_start(
                out=d_yp[J],
                in_=ysb[par][:, :].rearrange("l (c p) -> l c p", c=8)), st)
            pending = None
            ssm_tail_dv = dv.v

            _ck(B, nc, "groups")
            if DEBUG:
                dbg2 = [("dbg_ncum", ncum_tm, F32), ("dbg_atm", a_tm, F32),
                        ("dbg_dtbtm", dtb_tm, F32), ("dbg_wall", w_all, BF16),
                        ("dbg_exl", exl, F32), ("dbg_dec", dec, F32),
                        ("dbg_gt", gt_sb, BF16), ("dbg_xtm", xtm[(xidx - 1) % 2], BF16),
                        ("dbg_xdt", xdt[(xidx - 1) % 2], BF16),
                        ("dbg_mt0", mtb[0], BF16), ("dbg_exh0", exh[0], BF16),
                        ("dbg_muh", muh, F32), ("dbg_rh", rh, F32),
                        ("dbg_varh", varh, F32),
                        ("dbg_xcbf", xcbf, BF16), ("dbg_cfm", cfm, BF16),
                        ("dbg_bfm", bfm, BF16), ("dbg_cvacc", cv_acc, F32),
                        ("dbg_cvext", cv_ext, F32), ("dbg_xhln", xhln, BF16)]
                B.wait("g", dv)
                for nm, sb_t, dt in dbg2:
                    shp = list(sb_t.shape) if hasattr(sb_t, "shape") else None
                    o = nc.dram_tensor(nm, [128, sb_t.shape[1]], dt, kind="ExternalOutput").ap()
                    B.op("g", lambda E, o=o, sb_t=sb_t: E.dma_start(out=o, in_=sb_t[:, :]), st)
                o = nc.dram_tensor("dbg_sall", [128, INTER], F32, kind="ExternalOutput").ap()
                B.op("g", lambda E, o=o: E.dma_start(out=o, in_=s_all[:, :]), st)
            B.wait("g", dv, ssm_tail_dv)
            B.op("g", lambda E: E.dma_start(out=d_cci, in_=s_all[:, :]), st)
            s_st = st.v
            B.wait("g", st, s_st)
            B.op("g", lambda E: E.collective_compute(
                "AllGather", ALU.bypass,
                replica_groups=[[0, 1], [2, 3], [4, 5], [6, 7]],
                ins=[d_cci.opt()], outs=[d_cco.opt()]), cc)
            B.wait("g", cc)
            B.op("g", lambda E: E.dma_start(out=s_all[:, :], in_=d_cco[0:128, :]), ld)
            sin_ld = ld.v

            _ck(B, nc, "cc")
            gate_st = [0, 0]
            ystores_done = st.v
            pAB_free_gate = pAB_free
            for i in range(64):
                mi = 81 + i
                par = i % 2
                proj_matmuls(mi, halo=False)
                load_slab(mi + 2)
                gp = pe.v
                B.wait("a", pe, gp)
                B.wait("a", st, max(gate_st[par], ystores_done))
                B.op("a", lambda E, par=par: E.activation(gsb[par][:, 0:512], pA[:, :], AF.Copy))
                B.op("a", lambda E, par=par: E.activation(gsb[par][:, 512:T], pB[:, :], AF.Copy),
                     inc=ac)
                g_ac = ac.v
                pAB_free[0], pAB_free[1] = ac, g_ac
                B.wait("g", ac, g_ac)
                B.op("g", lambda E, i=i, par=par: E.dma_start(
                    out=d_gate[i], in_=gsb[par][:, 0:T]), st)
                gate_st[par] = st.v
            B.flush(nc)

        _ck(B, nc, "gate")
        with ExitStack() as PS:
            ep = PS.enter_context
            sinb = ep(nc.sbuf_tensor("sinb", [128, INTER], BF16))
            esca = ep(nc.sbuf_tensor("esca", [128, T], F32))
            ypb = [ep(nc.sbuf_tensor(f"ypb{i}", [128, T], F16)) for i in range(2)]
            yfb = [ep(nc.sbuf_tensor(f"yfb{i}", [128, T], F32)) for i in range(2)]
            sqscr = ep(nc.sbuf_tensor("sqscr", [128, 128], BF16))
            sq8 = ep(nc.sbuf_tensor("sq8", [128, 8], F32))
            ssqB = ep(nc.sbuf_tensor("ssqB", [128, 8], F32))
            scr8 = ep(nc.sbuf_tensor("scr8", [128, 8], F32))
            scr4a = ep(nc.sbuf_tensor("scr4a", [128, 128], F32))
            pC = [ep(nc.psum_tensor(f"pC{i}", [128, 512], F32)) for i in range(2)]
            pE4 = ep(nc.psum_tensor("pE4", [128, 512], F32))

            B.op("v", lambda E: E.memset(ssqB[:, :], 0.0))
            B.op("v", lambda E: E.memset(pxn[:, 0:1], 0.0))
            B.op("v", lambda E: E.tensor_tensor_scan(
                scr8[:, :], nct[:, :], nct[:, :], 0.0, op0=ALU.add, op1=ALU.bypass))
            B.op("v", lambda E: E.tensor_copy(zerot[:, 0:64], zerot[:, 64:128]))
            B.op("v", lambda E: E.tensor_copy(zerot[:, 0:64], zerot[:, 64:128]))
            B.op("v", lambda E: E.tensor_copy(pxn[:, 1:8], scr8[:, 0:7]), dv)
            B.wait("v", ld, sin_ld)
            B.op("v", lambda E: E.tensor_scalar(
                sinb[:, :], s_all[:, :], flagc[:, :], None, ALU.mult), dv)
            sin_dv = dv.v
            for c in range(8):
                cl = c * 128
                B.wait("t", dv, sin_dv)
                B.op("t", lambda E, c=c: E.matmul(
                    pE4[:, 0:128], pxn[:, c:c + 1].broadcast_to([128, 128]), idf[:, :],
                    start=True, stop=True), pe)
                B.wait("v", pe)
                B.op("v", lambda E, cl=cl: E.tensor_tensor(
                    scr4a[:, :], ncum_tm[:, cl:cl + 128], pE4[:, 0:128], ALU.add), dv)
                B.wait("a", dv)
                B.op("a", lambda E, cl=cl: E.activation(
                    esca[:, cl:cl + 128], scr4a[:, :], AF.Exp, scale=-1.0), ac)
                if c < 7:
                    B.wait("t", ac)
            esc_ac = ac.v

            yp_use = [0, 0]
            yf_st = [0, 0]
            for J in range(64):
                par = J % 2
                g = J // 8
                h0 = 2 * J
                B.wait("s", dv, yp_use[par])
                B.op("s", lambda E, J=J, par=par: E.dma_start(
                    out=ypb[par][:, :].rearrange("l (c p) -> l c p", c=8),
                    in_=d_yp[J]), ld)
                yp_ld = ld.v
                B.wait("v", ld, yp_ld)
                B.wait("v", st, yf_st[par])
                if J == 0:
                    B.wait("t", ac, esc_ac)
                for c in range(8):
                    cl = c * 128
                    bank = pC[c % 2]
                    B.wait("t", dv)
                    for hh in range(2):
                        B.op("t", lambda E, bank=bank, g=g, cl=cl, h0=h0, hh=hh: E.matmul(
                            bank[:, 64 * hh:64 * hh + 64],
                            cfm[:, g * T + cl:g * T + cl + 128],
                            sinb[:, 64 * (h0 + hh):64 * (h0 + hh) + 64],
                            start=True, stop=True), inc=(pe if hh == 1 else None))
                    yo_pe = pe.v
                    B.wait("v", pe, yo_pe)
                    for hh in range(2):
                        q = 64 * hh
                        B.op("v", lambda E, bank=bank, par=par, q=q, cl=cl, h0=h0, hh=hh:
                             E.scalar_tensor_tensor(
                                 yfb[par][:, cl + q:cl + q + 64], bank[:, q:q + 64],
                                 esca[:, cl + h0 + hh:cl + h0 + hh + 1],
                                 ypb[par][:, cl + q:cl + q + 64],
                                 op0=ALU.mult, op1=ALU.add),
                             inc=(dv if hh == 1 else None))
                    B.wait("a", dv)
                    B.op("a", lambda E, par=par, cl=cl, c=c: E.activation(
                        sqscr[:, :], yfb[par][:, cl:cl + 128], AF.Square,
                        accum_out=sq8[:, c:c + 1]), ac)
                B.wait("v", ac)
                acc_src = ssq_tm if J % 2 == 0 else ssqB
                acc_dst = ssqB if J % 2 == 0 else ssq_tm
                B.op("v", lambda E, a=acc_src, b=acc_dst: E.tensor_tensor(
                    b[:, :], a[:, :], sq8[:, :], ALU.add), dv)
                yp_use[par] = dv.v
                B.wait("g", dv)
                B.op("g", lambda E, J=J, par=par: E.dma_start(
                    out=d_yf[J].transpose([1, 0, 2]),
                    in_=yfb[par][:, :].rearrange("l (c p) -> l c p", c=8)), st)
                yf_st[par] = st.v
            p4a_st = st.v
            B.flush(nc)
        MS.close()
        _ck(B, nc, "p4a")

        with ExitStack() as PS:
            ep = PS.enter_context
            yg = ep(nc.sbuf_tensor("yg", [128, 64 * T], BF16))
            rs_bc = ep(nc.sbuf_tensor("rs_bc", [128, T], F32))
            ytr = [ep(nc.sbuf_tensor(f"ytr{i}", [128, T], F16)) for i in range(2)]
            gbuf = [ep(nc.sbuf_tensor(f"gbuf{i}", [128, T], F16)) for i in range(2)]
            rrow = ep(nc.sbuf_tensor("rrow", [1, 128], F32))
            r8 = ep(nc.sbuf_tensor("r8", [128, 8], F32))
            r8b = ep(nc.sbuf_tensor("r8b", [128, 8], F32))
            wsl2 = [ep(nc.sbuf_tensor(f"wsl2{i}", [128, 64 * 128], BF16)) for i in range(2)]
            mo_f = ep(nc.sbuf_tensor("mo_f", [128, T], F32))
            mo_bf = ep(nc.sbuf_tensor("mo_bf", [128, T], BF16))
            mo_sq = ep(nc.sbuf_tensor("mo_sq", [128, T], BF16))
            tscr = mo_f
            sgb = wsl2[0][:, 0:2048].bitcast(F32)
            pM0 = ep(nc.psum_tensor("pM0", [128, 512], F32))
            pM1 = ep(nc.psum_tensor("pM1", [128, 512], F32))
            pR = ep(nc.psum_tensor("pR", [128, 512], F32))
            pR2 = ep(nc.psum_tensor("pR2", [128, 512], F32))
            q_s0 = ep(nc.psum_tensor("q_s0", [128, 512], F32))
            q_s1 = ep(nc.psum_tensor("q_s1", [128, 512], F32))
            q_q0 = ep(nc.psum_tensor("q_q0", [128, 512], F32))
            q_q1 = ep(nc.psum_tensor("q_q1", [128, 512], F32))

            vsp()
            B.op("v", lambda E: E.tensor_scalar(
                r8[:, :], ssq_tm[:, :], 1.0 / INTER, None, ALU.mult), dv)
            B.wait("a", dv)
            B.op("a", lambda E: E.activation(r8b[:, :], r8[:, :], AF.Ln, bias=epscol[:, :]))
            B.op("a", lambda E: E.activation(r8[:, :], r8b[:, :], AF.Exp, scale=-0.5), ac)
            rstd_ac = ac.v
            B.wait("t", ac, rstd_ac)
            for c in range(8):
                if c > 0:
                    B.wait("t", dv)
                B.op("t", lambda E, c=c: E.transpose(
                    pR[0:1, 0:128], r8[:, c:c + 1], idf[:, :]), pe)
                B.wait("v", pe)
                B.op("v", lambda E: E.tensor_copy(rrow[:, :], pR[0:1, 0:128]), dv)
                B.wait("t", dv)
                B.op("t", lambda E: E.matmul(pR2[:, 0:128], o1r[:, :], rrow[:, :],
                                             start=True, stop=True), pe)
                B.wait("v", pe)
                B.op("v", lambda E, c=c: E.tensor_copy(
                    rs_bc[:, c * 128:(c + 1) * 128], pR2[:, 0:128]), dv)
            rs_dv = dv.v

            yg_use = [0, 0]
            for j in range(64):
                par = j % 2
                if j == 0:
                    B.wait("s", st, p4a_st)
                B.wait("s", dv, yg_use[par])
                B.op("s", lambda E, j=j, par=par: E.dma_start(
                    out=ytr[par][:, :],
                    in_=d_yf[j].rearrange("c l p -> (c l) p"), transpose=True), tr)
                ytr_tr = tr.v
                B.op("s", lambda E, j=j, par=par: E.dma_start(
                    out=gbuf[par][:, :], in_=d_gate[j]), ld)
                g_ld = ld.v
                B.wait("a", ld, g_ld)
                B.wait("a", dv)
                B.op("a", lambda E, par=par: E.activation(sgb[:, 0:T], gbuf[par][:, :], AF.Silu), ac)
                sg_ac = ac.v
                B.wait("v", ac, sg_ac)
                B.wait("v", tr, ytr_tr)
                B.wait("v", dv, rs_dv)
                B.op("v", lambda E, j=j, par=par: E.scalar_tensor_tensor(
                    tscr[:, :], ytr[par][:, :], nwc[:, j:j + 1], rs_bc[:, :],
                    op0=ALU.mult, op1=ALU.mult))
                B.op("v", lambda E, j=j: E.tensor_tensor(
                    yg[:, j * T:(j + 1) * T], tscr[:, :], sgb[:, 0:T], ALU.mult), dv)
                yg_use[par] = dv.v
            yg_dv = dv.v

            sl_free = [0, 0]
            sl_ld = [0, 0]

            def load_slab2(mi):
                if mi >= 32:
                    return
                buf = mi % 2
                B.wait("g", pe, sl_free[buf])
                B.op("g", lambda E, mi=mi, buf=buf: E.dma_start(
                    out=wsl2[buf][:, :].rearrange("p (k m) -> p k m", k=64),
                    in_=d_wout[mi].transpose([1, 0, 2])), wd)
                sl_ld[buf] = wd.v

            B.wait("g", dv, yg_dv)
            load_slab2(0)
            load_slab2(1)
            B.wait("t", dv, yg_dv)
            ev_dv = 0
            st_mo = 0
            for mi in range(32):
                buf = mi % 2
                B.wait("t", wd, sl_ld[buf])
                if mi > 0:
                    B.wait("t", dv, ev_dv)
                for k in range(64):
                    s0, s1 = (k == 0), (k == 63)
                    B.op("t", lambda E, k=k, buf=buf, s0=s0, s1=s1: E.matmul(
                        pM0[:, :], wsl2[buf][:, k * 128:(k + 1) * 128],
                        yg[:, k * T:k * T + 512], start=s0, stop=s1))
                    B.op("t", lambda E, k=k, buf=buf, s0=s0, s1=s1: E.matmul(
                        pM1[:, :], wsl2[buf][:, k * 128:(k + 1) * 128],
                        yg[:, k * T + 512:(k + 1) * T], start=s0, stop=s1),
                        inc=(pe if s1 else None))
                sl_free[buf] = pe.v
                mm_pe = pe.v
                load_slab2(mi + 2)
                B.wait("v", pe, mm_pe)
                B.wait("v", st, st_mo)
                B.op("v", lambda E: E.tensor_copy(mo_f[:, 0:512], pM0[:, :]))
                B.op("v", lambda E: E.tensor_copy(mo_f[:, 512:T], pM1[:, :]))
                B.op("v", lambda E: E.tensor_copy(mo_bf[:, 0:512], pM0[:, :]))
                B.op("v", lambda E: E.tensor_copy(mo_bf[:, 512:T], pM1[:, :]), inc=dv)
                ev_dv = dv.v
                B.wait("a", dv, ev_dv)
                B.op("a", lambda E: E.activation(mo_sq[:, :], mo_bf[:, :], AF.Square), ac)
                msq_ac = ac.v
                B.wait("t", ac, msq_ac)
                s0, s1 = (mi == 0), (mi == 31)
                B.op("t", lambda E, s0=s0, s1=s1: E.matmul(
                    q_s0[:, :], ones[:, :], mo_bf[:, 0:512], start=s0, stop=s1))
                B.op("t", lambda E, s0=s0, s1=s1: E.matmul(
                    q_s1[:, :], ones[:, :], mo_bf[:, 512:T], start=s0, stop=s1))
                B.op("t", lambda E, s0=s0, s1=s1: E.matmul(
                    q_q0[:, :], ones[:, :], mo_sq[:, 0:512], start=s0, stop=s1))
                B.op("t", lambda E, s0=s0, s1=s1: E.matmul(
                    q_q1[:, :], ones[:, :], mo_sq[:, 512:T], start=s0, stop=s1), inc=pe)
                B.wait("s", dv, ev_dv)
                B.op("s", lambda E, mi=mi: E.dma_start(out=d_mo[mi], in_=mo_f[:, :]), st)
                st_mo = st.v
            ln2_pe = pe.v
            B.wait("v", pe, ln2_pe)
            ln2_ac = LNfin((q_s0, q_s1, q_q0, q_q1))
            B.flush(nc)

        _ck(B, nc, "p5")
        with ExitStack() as PS:
            ep = PS.enter_context
            x0b = [ep(nc.sbuf_tensor(f"x0b{i}", [128, T], F32)) for i in range(2)]
            mob = [ep(nc.sbuf_tensor(f"mob{i}", [128, T], F32)) for i in range(2)]
            x1b = [ep(nc.sbuf_tensor(f"x1b{i}", [128, T], F32)) for i in range(2)]
            tb = ep(nc.sbuf_tensor("tb6", [128, T], F32))
            tb2 = ep(nc.sbuf_tensor("tb6b", [128, T], F32))
            xbf = ep(nc.sbuf_tensor("xbf6", [128, T], BF16))
            xsq = ep(nc.sbuf_tensor("xsq6", [128, T], BF16))
            u_s0 = ep(nc.psum_tensor("u_s0", [128, 512], F32))
            u_s1 = ep(nc.psum_tensor("u_s1", [128, 512], F32))
            u_q0 = ep(nc.psum_tensor("u_q0", [128, 512], F32))
            u_q1 = ep(nc.psum_tensor("u_q1", [128, 512], F32))

            st_x1 = [0, 0]
            for k in range(KT):
                par = k % 2
                B.wait("s", st, st_x1[par])
                B.op("s", lambda E, k=k, par=par: E.dma_start(
                    out=x0b[par][:, :], in_=d_x0[128 * k:128 * (k + 1), :]), ld)
                B.op("s", lambda E, k=k, par=par: E.dma_start(
                    out=mob[par][:, :], in_=d_mo[k]), ld)
                k_ld = ld.v
                B.wait("v", ld, k_ld)
                if k == 0:
                    B.wait("v", ac, ln2_ac)
                else:
                    B.wait("v", pe)
                B.op("v", lambda E, par=par: E.tensor_tensor(
                    tb[:, :], mob[par][:, :], mu_bc[:, :], ALU.subtract))
                B.op("v", lambda E, par=par: E.tensor_tensor(
                    tb2[:, :], tb[:, :], r_bc[:, :], ALU.mult))
                B.op("v", lambda E, par=par: E.tensor_tensor(
                    x1b[par][:, :], tb2[:, :], x0b[par][:, :], ALU.add))
                B.op("v", lambda E, par=par: E.tensor_copy(xbf[:, :], x1b[par][:, :]), inc=dv)
                k_dv = dv.v
                B.wait("a", dv, k_dv)
                B.op("a", lambda E: E.activation(xsq[:, :], xbf[:, :], AF.Square), ac)
                k_ac = ac.v
                B.wait("t", ac, k_ac)
                s0, s1 = (k == 0), (k == KT - 1)
                B.op("t", lambda E, s0=s0, s1=s1: E.matmul(
                    u_s0[:, :], ones[:, :], xbf[:, 0:512], start=s0, stop=s1))
                B.op("t", lambda E, s0=s0, s1=s1: E.matmul(
                    u_s1[:, :], ones[:, :], xbf[:, 512:T], start=s0, stop=s1))
                B.op("t", lambda E, s0=s0, s1=s1: E.matmul(
                    u_q0[:, :], ones[:, :], xsq[:, 0:512], start=s0, stop=s1))
                B.op("t", lambda E, s0=s0, s1=s1: E.matmul(
                    u_q1[:, :], ones[:, :], xsq[:, 512:T], start=s0, stop=s1), inc=pe)
                B.wait("s", dv, k_dv)
                B.op("s", lambda E, k=k, par=par: E.dma_start(
                    out=d_x1[k], in_=x1b[par][:, :]), st)
                st_x1[par] = st.v
            ln3_pe = pe.v
            B.wait("v", pe, ln3_pe)
            ln3_ac = LNfin((u_s0, u_s1, u_q0, u_q1))
            B.flush(nc)

        _ck(B, nc, "p6a")
        with ExitStack() as PS:
            ep = PS.enter_context
            bbf = ep(nc.sbuf_tensor("bbf", [128, KT * T], BF16))
            lbe = [ep(nc.sbuf_tensor(f"lbe{e}", [8, DIM], BF16)) for e in range(4)]
            zme = ep(nc.sbuf_tensor("zme", [8, 4 * T], BF16))
            zs = ep(nc.sbuf_tensor("zs", [128, T], F32))
            zs3 = ep(nc.sbuf_tensor("zs3", [8, T], F32))
            mk3 = ep(nc.sbuf_tensor("mk3", [8, T], BF16))
            lgs = ep(nc.sbuf_tensor("lgs", [4, T], F32))
            lgs2 = ep(nc.sbuf_tensor("lgs2", [4, T], F32))
            lgt = ep(nc.sbuf_tensor("lgt", [128, 4], F32))
            cmp1 = ep(nc.sbuf_tensor("cmp1", [128, 4], F32))
            cmp2 = ep(nc.sbuf_tensor("cmp2", [128, 4], F32))
            cmp3 = ep(nc.sbuf_tensor("cmp3", [128, 4], F32))
            mfm = ep(nc.sbuf_tensor("mfm", [4, T], BF16))
            mk_sb = ep(nc.sbuf_tensor("mk_sb", [128, T], BF16))
            wsl3 = [ep(nc.sbuf_tensor(f"wsl3{i}", [128, KT * 128], BF16)) for i in range(2)]
            fo_f = ep(nc.sbuf_tensor("fo_f", [128, T], F32))
            fo_bf = ep(nc.sbuf_tensor("fo_bf", [128, T], BF16))
            fo_sq = ep(nc.sbuf_tensor("fo_sq", [128, T], BF16))

            LS = ExitStack()
            x1t = [LS.enter_context(nc.sbuf_tensor(f"x1t{i}", [128, T], F32)) for i in range(2)]
            tb = LS.enter_context(nc.sbuf_tensor("tb7", [128, T], F32))
            lab = LS.enter_context(nc.sbuf_tensor("lab", [128, KT * 128], BF16))
            lgb = LS.enter_context(nc.sbuf_tensor("lgb", [128, KT * 4], BF16))
            lab3 = LS.enter_context(nc.sbuf_tensor("lab3", [128, KT * 8], BF16))
            for k in range(KT):
                par = k % 2
                if k >= 2:
                    B.wait("s", dv)
                B.op("s", lambda E, k=k, par=par: E.dma_start(
                    out=x1t[par][:, :], in_=d_x1[k]), ld)
                k_ld = ld.v
                B.wait("v", ld, k_ld)
                if k == 0:
                    B.wait("v", ac, ln3_ac)
                B.op("v", lambda E, par=par: E.tensor_tensor(
                    tb[:, :], x1t[par][:, :], mu_bc[:, :], ALU.subtract))
                B.op("v", lambda E, k=k: E.tensor_tensor(
                    bbf[:, k * T:(k + 1) * T], tb[:, :], r_bc[:, :], ALU.mult), inc=dv)
            bbf_dv = dv.v
            _ck(B, nc, "p6b")

            B.op("g", lambda E: E.dma_start(
                out=lab[:, :].rearrange("p (k m) -> p k m", k=KT),
                in_=d_lA.transpose([1, 0, 2])), wd)
            B.op("g", lambda E: E.dma_start(
                out=lgb[:, :].rearrange("p (k m) -> p k m", k=KT),
                in_=d_lgw.transpose([1, 0, 2])), wd)
            for e in range(4):
                B.op("g", lambda E, e=e: E.dma_start(out=lbe[e][:, :], in_=d_lB[e]), wd)
            B.op("g", lambda E: E.dma_start(
                out=lab3[:, :].rearrange("p (k m) -> p k m", k=KT),
                in_=d_lA3.transpose([1, 0, 2])), wd)
            lw_ld = wd.v

            with ExitStack() as ZS:
                ez = ZS.enter_context
                pz0 = ez(nc.psum_tensor("pz0", [128, 512], F32))
                pz1 = ez(nc.psum_tensor("pz1", [128, 512], F32))
                pl0 = ez(nc.psum_tensor("pl0", [4, 512], F32))
                pl1 = ez(nc.psum_tensor("pl1", [4, 512], F32))
                p30 = ez(nc.psum_tensor("p30", [8, 512], F32))
                p31 = ez(nc.psum_tensor("p31", [8, 512], F32))
                B.wait("t", dv, bbf_dv)
                B.wait("t", wd, lw_ld)
                for k in range(KT):
                    s0, s1 = (k == 0), (k == KT - 1)
                    B.op("t", lambda E, k=k, s0=s0, s1=s1: E.matmul(
                        pz0[:, :], lab[:, k * 128:(k + 1) * 128],
                        bbf[:, k * T:k * T + 512], start=s0, stop=s1))
                    B.op("t", lambda E, k=k, s0=s0, s1=s1: E.matmul(
                        pz1[:, :], lab[:, k * 128:(k + 1) * 128],
                        bbf[:, k * T + 512:(k + 1) * T], start=s0, stop=s1))
                    B.op("t", lambda E, k=k, s0=s0, s1=s1: E.matmul(
                        pl0[:, :], lgb[:, 4 * k:4 * k + 4],
                        bbf[:, k * T:k * T + 512], start=s0, stop=s1))
                    B.op("t", lambda E, k=k, s0=s0, s1=s1: E.matmul(
                        pl1[:, :], lgb[:, 4 * k:4 * k + 4],
                        bbf[:, k * T + 512:(k + 1) * T], start=s0, stop=s1))
                    B.op("t", lambda E, k=k, s0=s0, s1=s1: E.matmul(
                        p30[:, :], lab3[:, 8 * k:8 * k + 8],
                        bbf[:, k * T:k * T + 512], start=s0, stop=s1))
                    B.op("t", lambda E, k=k, s0=s0, s1=s1: E.matmul(
                        p31[:, :], lab3[:, 8 * k:8 * k + 8],
                        bbf[:, k * T + 512:(k + 1) * T], start=s0, stop=s1),
                        inc=(pe if s1 else None))
                z_pe = pe.v
                B.wait("v", pe, z_pe)
                B.op("v", lambda E: E.tensor_copy(zs[:, 0:512], pz0[:, :]))
                B.op("v", lambda E: E.tensor_copy(zs[:, 512:T], pz1[:, :]))
                B.op("v", lambda E: E.tensor_copy(lgs[:, 0:512], pl0[:, :]))
                B.op("v", lambda E: E.tensor_copy(lgs[:, 512:T], pl1[:, :]))
                B.op("v", lambda E: E.tensor_copy(zs3[:, 0:512], p30[:, :]))
                B.op("v", lambda E: E.tensor_copy(zs3[:, 512:T], p31[:, :]))
                B.op("v", lambda E: E.tensor_scalar(
                    lgs2[:, :], lgs[:, :], ebc[:, :], None, ALU.add), dv)
                lg_dv = dv.v
                _ck(B, nc, "z")
                pt7 = ez(nc.psum_tensor("pt7", [128, 512], F32))
                pt8 = ez(nc.psum_tensor("pt8", [128, 512], F32))
                B.wait("t", dv, lg_dv)
                for c in range(8):
                    cl = c * 128
                    if c > 0:
                        B.wait("t", dv)
                    B.op("t", lambda E, cl=cl: E.transpose(
                        pt7[:, 0:4], lgs2[0:4, cl:cl + 128], idf[0:4, 0:4]), pe)
                    B.wait("v", pe)
                    B.op("v", lambda E: E.tensor_copy(lgt[:, :], pt7[:, 0:4]))
                    B.op("v", lambda E: E.tensor_copy(zerot[:, 0:64], zerot[:, 64:128]))
                    specs = [
                        (0, (1, ALU.is_ge), (2, ALU.is_ge), (3, ALU.is_ge)),
                        (1, (0, ALU.is_gt), (2, ALU.is_ge), (3, ALU.is_ge)),
                        (2, (0, ALU.is_gt), (1, ALU.is_gt), (3, ALU.is_ge)),
                        (3, (0, ALU.is_gt), (1, ALU.is_gt), (2, ALU.is_gt)),
                    ]
                    for e, (o0, op0), _x1_, _x2_ in specs:
                        B.op("v", lambda E, e=e, o0=o0, op0=op0: E.tensor_tensor(
                            cmp1[:, e:e + 1], lgt[:, e:e + 1], lgt[:, o0:o0 + 1], op0))
                    for e, _x0_, (o1, op1), _x2_ in specs:
                        B.op("v", lambda E, e=e, o1=o1, op1=op1: E.tensor_tensor(
                            cmp2[:, e:e + 1], lgt[:, e:e + 1], lgt[:, o1:o1 + 1], op1))
                    for e, _x0_, _x1_, _x2_ in specs:
                        B.op("v", lambda E, e=e: E.tensor_tensor(
                            cmp3[:, e:e + 1], cmp1[:, e:e + 1], cmp2[:, e:e + 1], ALU.mult))
                    for e, _x0_, _x1_, (o2, op2) in specs:
                        B.op("v", lambda E, e=e, o2=o2, op2=op2: E.tensor_tensor(
                            cmp2[:, e:e + 1], lgt[:, e:e + 1], lgt[:, o2:o2 + 1], op2))
                    for e, _x0_, _x1_, _x2_ in specs:
                        B.op("v", lambda E, e=e: E.tensor_tensor(
                            cmp1[:, e:e + 1], cmp3[:, e:e + 1], cmp2[:, e:e + 1], ALU.mult),
                            inc=(dv if e == 3 else None))
                    m_dv = dv.v
                    B.wait("t", dv, m_dv)
                    B.op("t", lambda E: E.transpose(pt8[0:4, 0:128], cmp1[:, :], idf[:, :]), pe)
                    B.wait("v", pe)
                    B.op("v", lambda E, cl=cl: E.tensor_copy(
                        mfm[:, cl:cl + 128], pt8[0:4, 0:128]), dv)
                mfm_dv = dv.v
                _ck(B, nc, "argmax")
                B.wait("t", dv, mfm_dv)
                B.op("t", lambda E: E.matmul(pt7[:, 0:512], erep[:, :], mfm[:, 0:512],
                                             start=True, stop=True))
                B.op("t", lambda E: E.matmul(pt8[:, 0:512], erep[:, :], mfm[:, 512:T],
                                             start=True, stop=True))
                B.op("t", lambda E: E.matmul(p30[:, :], erep3[:, :], mfm[:, 0:512],
                                             start=True, stop=True))
                B.op("t", lambda E: E.matmul(p31[:, :], erep3[:, :], mfm[:, 512:T],
                                             start=True, stop=True), inc=pe)
                B.wait("v", pe)
                B.op("v", lambda E: E.tensor_copy(mk_sb[:, 0:512], pt7[:, 0:512]))
                B.op("v", lambda E: E.tensor_copy(mk_sb[:, 512:T], pt8[:, 0:512]))
                B.op("v", lambda E: E.tensor_copy(mk3[:, 0:512], p30[:, :]))
                B.op("v", lambda E: E.tensor_copy(mk3[:, 512:T], p31[:, :]))
                for e in range(3):
                    B.op("v", lambda E, e=e: E.tensor_tensor(
                        zme[:, e * T:(e + 1) * T], zs[32 * e:32 * e + 8, :],
                        mk_sb[32 * e:32 * e + 8, :], ALU.mult))
                B.op("v", lambda E: E.tensor_tensor(
                    zme[:, 3 * T:4 * T], zs3[:, :], mk3[:, :], ALU.mult), inc=dv)
                zm_dv = dv.v
            LS.close()
            _ck(B, nc, "zm")

            sl_free3 = [0, 0]
            sl_ld3 = [0, 0]

            def load_slab3(mi):
                if mi >= 32:
                    return
                buf = mi % 2
                B.wait("g", pe, sl_free3[buf])
                B.op("g", lambda E, mi=mi, buf=buf: E.dma_start(
                    out=wsl3[buf][:, :].rearrange("p (k m) -> p k m", k=KT),
                    in_=d_wup[mi].transpose([1, 0, 2])), wd)
                sl_ld3[buf] = wd.v

            with ExitStack() as FS:
                ef = FS.enter_context
                pF0 = ef(nc.psum_tensor("pF0", [128, 512], F32))
                pF1 = ef(nc.psum_tensor("pF1", [128, 512], F32))
                pD0 = ef(nc.psum_tensor("pD0", [128, 512], F32))
                pD1 = ef(nc.psum_tensor("pD1", [128, 512], F32))
                dsc = ef(nc.sbuf_tensor("dsc", [128, T], F32))
                v_s0 = ef(nc.psum_tensor("v_s0", [128, 512], F32))
                v_s1 = ef(nc.psum_tensor("v_s1", [128, 512], F32))
                v_q0 = ef(nc.psum_tensor("v_q0", [128, 512], F32))
                v_q1 = ef(nc.psum_tensor("v_q1", [128, 512], F32))
                load_slab3(0)
                load_slab3(1)
                B.wait("t", dv, zm_dv)
                ev_dv = 0
                st_ff = 0
                for mi in range(32):
                    buf = mi % 2
                    B.wait("t", wd, sl_ld3[buf])
                    if mi > 0:
                        B.wait("t", dv, ev_dv)
                    for k in range(KT):
                        s0, s1 = (k == 0), (k == KT - 1)
                        B.op("t", lambda E, k=k, buf=buf, s0=s0, s1=s1: E.matmul(
                            pF0[:, :], wsl3[buf][:, k * 128:(k + 1) * 128],
                            bbf[:, k * T:k * T + 512], start=s0, stop=s1))
                        B.op("t", lambda E, k=k, buf=buf, s0=s0, s1=s1: E.matmul(
                            pF1[:, :], wsl3[buf][:, k * 128:(k + 1) * 128],
                            bbf[:, k * T + 512:(k + 1) * T], start=s0, stop=s1))
                    for e in range(4):
                        s0, s1 = (e == 0), (e == 3)
                        B.op("t", lambda E, e=e, mi=mi, s0=s0, s1=s1: E.matmul(
                            pD0[:, :], lbe[e][:, mi * 128:(mi + 1) * 128],
                            zme[:, e * T:e * T + 512], start=s0, stop=s1))
                        B.op("t", lambda E, e=e, mi=mi, s0=s0, s1=s1: E.matmul(
                            pD1[:, :], lbe[e][:, mi * 128:(mi + 1) * 128],
                            zme[:, e * T + 512:(e + 1) * T], start=s0, stop=s1),
                            inc=(pe if s1 else None))
                    sl_free3[buf] = pe.v
                    mm_pe = pe.v
                    load_slab3(mi + 2)
                    B.wait("v", pe, mm_pe)
                    B.wait("v", st, st_ff)
                    B.op("v", lambda E: E.tensor_copy(dsc[:, 0:512], pD0[:, :]))
                    B.op("v", lambda E: E.tensor_copy(dsc[:, 512:T], pD1[:, :]))
                    B.op("v", lambda E: E.scalar_tensor_tensor(
                        fo_f[:, 0:512], dsc[:, 0:512], 1.0, pF0[:, :],
                        op0=ALU.mult, op1=ALU.add))
                    B.op("v", lambda E: E.scalar_tensor_tensor(
                        fo_f[:, 512:T], dsc[:, 512:T], 1.0, pF1[:, :],
                        op0=ALU.mult, op1=ALU.add))
                    B.op("v", lambda E: E.tensor_copy(fo_bf[:, 0:512], fo_f[:, 0:512]))
                    B.op("v", lambda E: E.tensor_copy(fo_bf[:, 512:T], fo_f[:, 512:T]), inc=dv)
                    ev_dv = dv.v
                    B.wait("a", dv, ev_dv)
                    B.op("a", lambda E: E.activation(fo_sq[:, :], fo_bf[:, :], AF.Square), ac)
                    msq_ac = ac.v
                    B.wait("t", ac, msq_ac)
                    s0, s1 = (mi == 0), (mi == 31)
                    B.op("t", lambda E, s0=s0, s1=s1: E.matmul(
                        v_s0[:, :], ones[:, :], fo_bf[:, 0:512], start=s0, stop=s1))
                    B.op("t", lambda E, s0=s0, s1=s1: E.matmul(
                        v_s1[:, :], ones[:, :], fo_bf[:, 512:T], start=s0, stop=s1))
                    B.op("t", lambda E, s0=s0, s1=s1: E.matmul(
                        v_q0[:, :], ones[:, :], fo_sq[:, 0:512], start=s0, stop=s1))
                    B.op("t", lambda E, s0=s0, s1=s1: E.matmul(
                        v_q1[:, :], ones[:, :], fo_sq[:, 512:T], start=s0, stop=s1), inc=pe)
                    B.wait("s", dv, ev_dv)
                    B.op("s", lambda E, mi=mi: E.dma_start(out=d_ff[mi], in_=fo_f[:, :]), st)
                    st_ff = st.v
                    if mi == 0:
                        _ck(B, nc, "ffn1")
                ln4_pe = pe.v
                B.wait("v", pe, ln4_pe)
                ln4_ac = LNfin((v_s0, v_s1, v_q0, v_q1))
            B.flush(nc)

        _ck(B, nc, "p7")
        with ExitStack() as PS:
            ep = PS.enter_context
            x1o = [ep(nc.sbuf_tensor(f"x1o{i}", [128, T], F32)) for i in range(2)]
            ffo = [ep(nc.sbuf_tensor(f"ffo{i}", [128, T], F32)) for i in range(2)]
            ob = [ep(nc.sbuf_tensor(f"ob{i}", [128, T], F32)) for i in range(2)]
            tb = ep(nc.sbuf_tensor("tb8", [128, T], F32))
            tb2 = ep(nc.sbuf_tensor("tb8b", [128, T], F32))

            st_o = [0, 0]
            for k in range(KT):
                par = k % 2
                B.wait("s", st, st_o[par])
                B.op("s", lambda E, k=k, par=par: E.dma_start(
                    out=x1o[par][:, :], in_=d_x1[k]), ld)
                B.op("s", lambda E, k=k, par=par: E.dma_start(
                    out=ffo[par][:, :], in_=d_ff[k]), ld)
                k_ld = ld.v
                B.wait("v", ld, k_ld)
                if k == 0:
                    B.wait("v", ac, ln4_ac)
                B.op("v", lambda E, par=par: E.tensor_tensor(
                    tb[:, :], ffo[par][:, :], mu_bc[:, :], ALU.subtract))
                B.op("v", lambda E, par=par: E.tensor_tensor(
                    tb2[:, :], tb[:, :], r_bc[:, :], ALU.mult))
                B.op("v", lambda E, par=par: E.tensor_tensor(
                    ob[par][:, :], tb2[:, :], x1o[par][:, :], ALU.add), inc=dv)
                k_dv = dv.v
                B.wait("s", dv, k_dv)
                B.op("s", lambda E, k=k, par=par: E.dma_start(
                    out=d_out[k], in_=ob[par][:, :]), st)
                st_o[par] = st.v
            B.wait("s", st)
            if DEBUG:
                for nm, sb_t in (("dbg_dtbtm", None),):
                    pass
                dbg = [("dbg_gate", d_gate, [64, 128, T], F16),
                       ("dbg_yp", d_yp, [64, 128, 8, 128], F16),
                       ("dbg_yf", d_yf, [64, 8, 128, 128], F16),
                       ("dbg_mo", d_mo, [KT, 128, T], F32),
                       ("dbg_x1", d_x1, [KT, 128, T], F32),
                       ("dbg_ff", d_ff, [KT, 128, T], F32),
                       ("dbg_cco", d_cco, [256, INTER], F32)]
                for nm, src, shp, dt in dbg:
                    o = nc.dram_tensor(nm, shp, dt, kind="ExternalOutput").ap()
                    B.op("g", lambda E, o=o, src=src: E.dma_start(out=o, in_=src), st)
                B.wait("g", st)
            B.flush(nc)

    return nc


_NC_CACHE = [None]


def _host_prep(inputs):
    W_in = np.ascontiguousarray(inputs["L0_attn_in_proj_w"], np.float32)
    W_out = np.ascontiguousarray(inputs["L0_attn_out_proj_w"], np.float32)
    W_up = np.ascontiguousarray(inputs["L0_ffn_W_up"], np.float32)
    order = np.concatenate([np.arange(b, b + 128) for (_, b, _) in MTILES])
    WinT = np.ascontiguousarray(
        W_in.T[:, order].reshape(KT, 128, 145, 128).transpose(2, 0, 1, 3))
    WoutT = np.ascontiguousarray(
        W_out.T.reshape(64, 128, 32, 128).transpose(2, 0, 1, 3))
    WupT = np.ascontiguousarray(
        W_up.T.reshape(KT, 128, 32, 128).transpose(2, 0, 1, 3))
    lA = np.asarray(inputs["L0_ffn_lora_A"], np.float32)
    lApad_full = np.zeros((DIM, 128), np.float32)
    for e in range(3):
        lApad_full[:, 32 * e:32 * e + 8] = lA[e].T
    lApad = np.ascontiguousarray(lApad_full.reshape(KT, 128, 128))
    lA3 = np.ascontiguousarray(lA[3].T.reshape(KT, 128, 8))
    lgw = np.ascontiguousarray(
        np.asarray(inputs["L0_ffn_gate_w"], np.float32).T.reshape(KT, 128, 4))
    lBT = np.ascontiguousarray(
        np.asarray(inputs["L0_ffn_lora_B"], np.float32).transpose(0, 2, 1))
    hbc_order = [b - 8192 for (kind, b, _) in MTILES if kind in ("B", "C", "x")]
    cw = np.asarray(inputs["L0_attn_conv_w"], np.float32)[:, 0, :]
    cb = np.asarray(inputs["L0_attn_conv_b"], np.float32)
    cw_t = np.stack([cw[b:b + 128] for b in hbc_order])
    cb_t = np.stack([cb[b:b + 128] for b in hbc_order])
    cw4 = np.ascontiguousarray(cw_t.transpose(1, 0, 2))
    cbT = np.ascontiguousarray(cb_t.T)
    nwT = np.ascontiguousarray(
        np.asarray(inputs["L0_attn_norm_w"], np.float32).reshape(64, 128).T)
    triU = np.triu(np.ones((128, 128), np.float32))
    identf = np.eye(128, dtype=np.float32)
    causal = (np.arange(128)[None, :] >= np.arange(128)[:, None]).astype(np.uint8)
    ones128 = np.ones((128, 128), np.float32)
    ones1r = np.ones((1, 128), np.float32)
    erep2 = np.zeros((4, 128), np.float32)
    for e in range(3):
        erep2[e, 32 * e:32 * e + 8] = 1.0
    erep3 = np.zeros((4, 8), np.float32)
    erep3[3, :] = 1.0
    common = dict(
        WinT=WinT, WoutT=WoutT, WupT=WupT, lApad=lApad, lgw=lgw, lBT=lBT,
        cw4=cw4, cbT=cbT,
        dtbias=np.asarray(inputs["L0_attn_dt_bias"], np.float32).reshape(128, 1),
        alog=np.asarray(inputs["L0_attn_A_log"], np.float32).reshape(128, 1),
        dvec=np.asarray(inputs["L0_attn_D"], np.float32).reshape(1, 128),
        nwT=nwT,
        ebias=np.asarray(inputs["L0_ffn_expert_bias"], np.float32).reshape(4, 1),
        triU=triU, identf=identf, causal=causal, ones128=ones128,
        ones1r=ones1r, erep2=erep2, lA3=lA3, erep3=erep3,
    )
    x0 = np.asarray(inputs["L0_attn_hidden_states"], np.float32)
    maps = []
    for c in range(8):
        b, h = c // 2, c % 2
        t0 = h * T
        m = dict(common)
        m["x0T"] = np.ascontiguousarray(x0[b, t0:t0 + T].T)
        if h == 1:
            m["x0hT"] = np.ascontiguousarray(x0[b, t0 - 4:t0].T)
        else:
            m["x0hT"] = np.zeros((DIM, 4), np.float32)
        m["flag"] = np.full((128, 1), float(h), np.float32)
        maps.append(m)
    return maps


def kernel(**inputs):
    maps = _host_prep(inputs)
    if _NC_CACHE[0] is None:
        _NC_CACHE[0] = build()
    nc = _NC_CACHE[0]
    res = run_bass_kernel_spmd(nc, maps, core_ids=list(range(8)))
    out = np.empty((4, 2048, DIM), np.float32)
    for c in range(8):
        b, h = c // 2, c % 2
        o = np.asarray(res.results[c]["o"], np.float32).reshape(DIM, T)
        out[b, h * T:(h + 1) * T, :] = o.T
    return out
